# version 04
# baseline (speedup 1.0000x reference)
import numpy as np
import ml_dtypes
from contextlib import ExitStack
import concourse.bass as bass
import concourse.mybir as mybir
from concourse.bass_utils import run_bass_kernel_spmd
from concourse.alu_op_type import AluOpType as OP

F32 = mybir.dt.float32
BF16 = mybir.dt.bfloat16
I32 = mybir.dt.int32
AF = mybir.ActivationFunctionType
AX = mybir.AxisListType
ENGINES = ("sync", "tensor", "vector", "scalar", "gpsimd")

NC = 8
S = 16384
D = 4096
T = 2048
DT = 32
HW = T + 512
CSHIFT = 60.0
SQRT512 = float(np.sqrt(512.0))
EPS = 1e-5
SEMS = ["ld", "st", "mm", "dve", "act", "gp", "cc"]
RG = [list(range(NC))]


class Prog:
    def __init__(self, nc, stack, sem_names=SEMS):
        self.nc = nc
        self.ops = {e: [] for e in ENGINES}
        self.sems = {n: stack.enter_context(nc.semaphore(n)) for n in sem_names}
        self.cnt = {n: 0 for n in sem_names}

    def on(self, eng, fn):
        self.ops[eng].append(fn)

    def wait(self, eng, sem, val=None):
        v = self.cnt[sem] if val is None else val
        s = self.sems[sem]
        self.on(eng, lambda e, s=s, v=v: e.wait_ge(s, v))

    def step(self, eng, builder, inc=None, amt=1):
        if inc is None:
            self.on(eng, builder)
            return None
        s = self.sems[inc]
        self.cnt[inc] += amt
        self.on(eng, lambda e, b=builder, s=s, a=amt: b(e).then_inc(s, a))
        return self.cnt[inc]

    def chain(self, eng, builder, inc):
        v = self.step(eng, builder, inc=inc)
        self.wait(eng, inc, v)
        return v

    def barrier(self, sems=("ld", "st", "mm", "dve", "act", "gp")):
        for e in ENGINES:
            for sn in sems:
                self.wait(e, sn)

    def finalize(self):
        with self.nc.Block() as block:
            for e in ENGINES:
                def emit(eng, _e=e):
                    for f in self.ops[_e]:
                        f(eng)
                getattr(block, e)(emit)


class Ar:
    CAP = {"f": 6720, "b": 25408, "i": 640}
    DTY = {"f": F32, "b": BF16, "i": I32}

    def __init__(self, nc, s):
        self.t = {k: s(nc.sbuf_tensor("ar_" + k, [128, self.CAP[k]], self.DTY[k]))
                  for k in self.CAP}
        self.off = {k: 0 for k in self.CAP}

    def al(self, k, p, w):
        o = (self.off[k] + 15) & ~15
        assert o + w <= self.CAP[k], ("arena overflow", k, o, w)
        self.off[k] = o + w
        return self.t[k][0:p, o:o + w]

    def reset(self):
        self.off = {k: 0 for k in self.CAP}


def build_fused():
    nc = bass.Bass("TRN2", target_bir_lowering=False)
    stack = ExitStack()
    s = stack.enter_context

    x = nc.dram_tensor("x", [T, D], F32, kind="ExternalInput").ap()
    wqkvT = nc.dram_tensor("wqkvT", [D, 1536], F32, kind="ExternalInput").ap()
    bias12 = nc.dram_tensor("bias12", [128, 12], F32, kind="ExternalInput").ap()
    qwr = nc.dram_tensor("qwr", [D, D], I32, kind="ExternalInput").ap()
    scz = nc.dram_tensor("scz", [128, 4 * DT], F32, kind="ExternalInput").ap()
    woT0 = nc.dram_tensor("woT0", [512, D], F32, kind="ExternalInput").ap()
    shifts = nc.dram_tensor("shifts", [128, 1], I32, kind="ExternalInput").ap()
    wqT1 = nc.dram_tensor("wqT1", [D, D], F32, kind="ExternalInput").ap()
    wkvT1 = nc.dram_tensor("wkvT1", [D, 256], F32, kind="ExternalInput").ap()
    cosK = nc.dram_tensor("cosK", [128, T], BF16, kind="ExternalInput").ap()
    sinKs = nc.dram_tensor("sinKs", [128, T], BF16, kind="ExternalInput").ap()
    sigb = nc.dram_tensor("sigb", [128, 64], F32, kind="ExternalInput").ap()
    alph = nc.dram_tensor("alph", [128, T], BF16, kind="ExternalInput").ap()
    hmask = nc.dram_tensor("hmask", [128, 16], F32, kind="ExternalInput").ap()
    woT1 = nc.dram_tensor("woT1", [D, D], F32, kind="ExternalInput").ap()
    wbT1 = nc.dram_tensor("wbT1", [D, D], F32, kind="ExternalInput").ap()
    bco = nc.dram_tensor("bco", [128, DT], F32, kind="ExternalInput").ap()
    wgT1 = nc.dram_tensor("wgT1", [D, 8], F32, kind="ExternalInput").ap()
    wtr = nc.dram_tensor("wtr", [128, DT], F32, kind="ExternalInput").ap()
    btr = nc.dram_tensor("btr", [1, 1], F32, kind="ExternalInput").ap()
    aaT = nc.dram_tensor("aaT", [D, 128], F32, kind="ExternalInput").ap()
    baT = nc.dram_tensor("baT", [128, D], F32, kind="ExternalInput").ap()
    eex = nc.dram_tensor("eex", [8, 128], BF16, kind="ExternalInput").ap()
    out_o = nc.dram_tensor("out_o", [T, D], F32, kind="ExternalOutput").ap()

    qT_t = nc.dram_tensor("qT_i", [512, T], BF16)
    xT_t = nc.dram_tensor("xT_i", [D, T], F32)
    kT_loc_t = nc.dram_tensor("kT_loc", [512, T], BF16)
    v_loc_t = nc.dram_tensor("v_loc", [T, 512], BF16)
    kT_all_t = nc.dram_tensor("kT_all", [NC * 512, T], BF16, addr_space="Shared")
    v_all_t = nc.dram_tensor("v_all", [S, 512], BF16, addr_space="Shared")
    x1T_t = nc.dram_tensor("x1T_i", [D, T], F32)
    xn2T_t = nc.dram_tensor("xn2T_i", [D, T], BF16)
    kv_loc_t = nc.dram_tensor("kv_loc", [256, T], BF16)
    kv_all_t = nc.dram_tensor("kv_all", [NC * 256, T], BF16, addr_space="Shared")
    acT_t = nc.dram_tensor("acT_i", [D, T], BF16)
    pt_t = nc.dram_tensor("pt_i", [D, T], F32)
    qT_i, xT_i = qT_t.ap(), xT_t.ap()
    kT_all, v_all = kT_all_t.ap(), v_all_t.ap()
    x1T_i, xn2T_i = x1T_t.ap(), xn2T_t.ap()
    kv_all, acT_i, pt_i = kv_all_t.ap(), acT_t.ap(), pt_t.ap()
    kT_loc, v_loc, kv_loc = kT_loc_t.ap(), v_loc_t.ap(), kv_loc_t.ap()

    big = s(nc.sbuf_tensor("big", [128, DT * T], BF16))
    ident = s(nc.sbuf_tensor("ident", [128, 128], BF16))
    ones_b = s(nc.sbuf_tensor("ones_b", [128, 1], BF16))
    ones_f = s(nc.sbuf_tensor("ones_f", [1, 128], F32))
    iota_f = s(nc.sbuf_tensor("iota_f", [128, 128], F32))
    iota_p = s(nc.sbuf_tensor("iota_p", [128, 1], F32))
    ar = Ar(nc, s)

    pP = s(nc.psum_tensor("pP", [128, 2048], F32))
    pS2a = s(nc.psum_tensor("pS2a", [128, 512], F32))
    pS2b = s(nc.psum_tensor("pS2b", [128, 512], F32))
    pXb1 = s(nc.psum_tensor("pXb1", [128, 512], BF16))
    pXb2 = s(nc.psum_tensor("pXb2", [128, 512], BF16))

    p = Prog(nc, stack)
    p.step("gpsimd", lambda e: e.iota(iota_f[:], [[1, 128]], channel_multiplier=0, allow_small_or_imprecise_dtypes=True), inc="gp")
    p.step("gpsimd", lambda e: e.iota(iota_p[:], [[0, 1]], channel_multiplier=1, allow_small_or_imprecise_dtypes=True), inc="gp")
    p.step("gpsimd", lambda e: e.memset(ones_b[:], 1.0), inc="gp")
    p.step("gpsimd", lambda e: e.memset(ones_f[:], 1.0), inc="gp")
    p.wait("vector", "gp")
    p.step("vector", lambda e: e.tensor_scalar(ident[:], iota_f[:], iota_p[:], None, OP.is_equal), inc="dve")

    xt = ar.al("f", 128, 4096)
    ttmp = ar.al("f", 128, 128)
    ttmp2 = ar.al("f", 128, 128)
    tlo = ar.al("f", 128, 128)
    wz = ar.al("f", 128, 2048)
    st1 = ar.al("f", 128, 16)
    bias_sb = ar.al("f", 128, 16)
    hi = ar.al("b", 128, 4096)
    lo = ar.al("b", 128, 4096)
    sq = ar.al("b", 128, 4096)
    xn = ar.al("b", 128, 4096)
    wb = [ar.al("b", 128, 4096) for _ in range(2)]
    ev = ar.al("b", 128, 512)
    vtk1 = ar.al("b", 128, 128)
    pA = pXb1[:, 0:128]
    pB = pXb2[:, 0:128]
    pC = pXb1[:, 256:384]

    p.step("sync", lambda e: e.dma_start(out=bias_sb[:, 0:12], in_=bias12), inc="ld", amt=16)

    for tt in range(16):
        vld = p.step("sync", lambda e, tt=tt: e.dma_start(out=xt[:], in_=x[tt * 128:(tt + 1) * 128, :]), inc="ld", amt=16)
        p.wait("vector", "ld", vld)
        p.chain("vector", lambda e: e.tensor_copy(hi[:], xt[:]), inc="dve")
        vl = p.chain("vector", lambda e: e.tensor_tensor(lo[:], xt[:], hi[:], OP.subtract), inc="dve")
        p.chain("vector", lambda e: e.reduce_sum(st1[:, 0:1], xt[:], axis=AX.X), inc="dve")
        p.wait("scalar", "ld", vld)
        a1 = p.step("scalar", lambda e: e.activation(sq[:], xt[:], AF.Square), inc="act")
        p.wait("vector", "act", a1)
        p.chain("vector", lambda e: e.reduce_sum(st1[:, 1:2], sq[:], axis=AX.X), inc="dve")
        p.chain("vector", lambda e: e.tensor_scalar(st1[:, 2:3], st1[:, 0:1], 1.0 / D, None, OP.mult), inc="dve")
        p.chain("vector", lambda e: e.tensor_tensor(st1[:, 3:4], st1[:, 2:3], st1[:, 2:3], OP.mult), inc="dve")
        v5 = p.chain("vector", lambda e: e.tensor_scalar(st1[:, 4:5], st1[:, 1:2], 1.0 / D, st1[:, 3:4], OP.mult, OP.subtract), inc="dve")
        p.wait("scalar", "dve", v5)
        a2 = p.step("scalar", lambda e: e.activation(st1[:, 5:6], st1[:, 4:5], AF.Sqrt, bias=np.float32(EPS)), inc="act")
        p.wait("vector", "act", a2)
        p.chain("vector", lambda e: e.reciprocal(st1[:, 6:7], st1[:, 5:6]), inc="dve")
        v7 = p.chain("vector", lambda e: e.tensor_scalar(xn[:], xt[:], st1[:, 2:3], st1[:, 6:7], OP.subtract, OP.mult), inc="dve")
        p.wait("tensor", "dve", v7)
        for dc in range(DT):
            mh = p.step("tensor", lambda e, dc=dc: e.transpose(pA, hi[:, dc * 128:(dc + 1) * 128], ident[:]), inc="mm")
            ml = p.step("tensor", lambda e, dc=dc: e.transpose(pB, lo[:, dc * 128:(dc + 1) * 128], ident[:]), inc="mm")
            p.wait("scalar", "mm", mh)
            a3 = p.step("scalar", lambda e: e.activation(ttmp[:], pA, AF.Copy), inc="act")
            p.wait("tensor", "act", a3)
            mx = p.step("tensor", lambda e, dc=dc: e.transpose(pC, xn[:, dc * 128:(dc + 1) * 128], ident[:]), inc="mm")
            p.wait("scalar", "mm", ml)
            a3b = p.step("scalar", lambda e: e.activation(tlo[:], pB, AF.Copy), inc="act")
            p.wait("vector", "act", a3b)
            v8 = p.step("vector", lambda e: e.tensor_tensor(ttmp2[:], ttmp[:], tlo[:], OP.add), inc="dve")
            p.wait("scalar", "mm", mx)
            a4 = p.step("scalar", lambda e, tt=tt, dc=dc: e.activation(big[:, dc * T + tt * 128: dc * T + (tt + 1) * 128], pC, AF.Copy), inc="act")
            p.wait("gpsimd", "dve", v8)
            p.step("gpsimd", lambda e, tt=tt, dc=dc: e.dma_start(out=xT_i[dc * 128:(dc + 1) * 128, tt * 128:(tt + 1) * 128], in_=ttmp2[:]), inc="st", amt=16)
            p.wait("tensor", "act", a4)
            p.wait("tensor", "dve", v8)
            p.wait("scalar", "st")
            p.wait("vector", "st")
        p.wait("sync", "dve")
        p.wait("sync", "act")
        p.wait("vector", "mm")

    wguard = {}
    for ot in range(12):
        par = ot % 2
        for half in range(2):
            for g in range(2):
                p.step("sync", lambda e, ot=ot, half=half, g=g: e.dma_start(
                    out=wz[:, g * 1024:(g + 1) * 1024].rearrange("p (c o) -> p c o", o=128),
                    in_=wqkvT[:, ot * 128:(ot + 1) * 128].rearrange("(c p) o -> p c o", p=128)[:, (half * 2 + g) * 8:(half * 2 + g + 1) * 8, :]), inc="ld", amt=16)
            lv = p.cnt["ld"]
            p.wait("scalar", "ld", lv)
            if half == 0 and par in wguard:
                p.wait("scalar", "mm", wguard[par])
            cv = p.step("scalar", lambda e, par=par, half=half: e.copy(wb[par][:, half * 2048:(half + 1) * 2048], wz[:]), inc="act")
            p.wait("sync", "act", cv)
        p.wait("tensor", "act", cv)
        for tc in range(4):
            for ic in range(DT):
                p.step("tensor", lambda e, par=par, ic=ic, tc=tc: e.matmul(
                    pP[:, tc * 512:(tc + 1) * 512], wb[par][:, ic * 128:(ic + 1) * 128],
                    big[:, ic * T + tc * 512: ic * T + tc * 512 + 512],
                    start=(ic == 0), stop=(ic == DT - 1)), inc="mm")
            mdone = p.cnt["mm"]
            p.wait("scalar", "mm", mdone)
            ae = p.step("scalar", lambda e, ot=ot, tc=tc: e.activation(ev[:, 0:512], pP[:, tc * 512:(tc + 1) * 512], AF.Identity, bias=bias_sb[:, ot:ot + 1]), inc="act")
            p.wait("gpsimd", "act", ae)
            if ot < 4:
                p.step("gpsimd", lambda e, ot=ot, tc=tc: e.dma_start(out=qT_i[ot * 128:(ot + 1) * 128, tc * 512:(tc + 1) * 512], in_=ev[:, 0:512]), inc="st", amt=16)
            elif ot < 8:
                p.step("gpsimd", lambda e, ot=ot, tc=tc: e.dma_start(out=kT_loc[(ot - 4) * 128:(ot - 3) * 128, tc * 512:(tc + 1) * 512], in_=ev[:, 0:512]), inc="st", amt=16)
            else:
                for j in range(4):
                    mt = p.step("tensor", lambda e, j=j: e.transpose(pC, ev[:, j * 128:(j + 1) * 128], ident[:]), inc="mm")
                    p.wait("scalar", "mm", mt)
                    av = p.step("scalar", lambda e: e.activation(vtk1[:], pC, AF.Copy), inc="act")
                    p.wait("gpsimd", "act", av)
                    p.step("gpsimd", lambda e, ot=ot, tc=tc, j=j: e.dma_start(
                        out=v_loc[tc * 512 + j * 128: tc * 512 + (j + 1) * 128, (ot - 8) * 128:(ot - 7) * 128], in_=vtk1[:]), inc="st", amt=16)
                    p.wait("tensor", "act", av)
                    p.wait("scalar", "st")
            p.wait("tensor", "act", p.cnt["act"])
            p.wait("scalar", "st")
        wguard[par] = p.cnt["mm"]

    p.wait("gpsimd", "st")
    p.step("gpsimd", lambda e: e.collective_compute(
        "AllGather", mybir.AluOpType.bypass, replica_groups=RG,
        ins=[kT_loc.opt()], outs=[kT_all.opt()]), inc="cc")
    p.step("gpsimd", lambda e: e.collective_compute(
        "AllGather", mybir.AluOpType.bypass, replica_groups=RG,
        ins=[v_loc.opt()], outs=[v_all.opt()]), inc="cc")

    p.barrier()
    ar.reset()

    rs_row = ar.al("f", 1, 2048)
    l0 = ar.al("f", 1, 512)
    l1 = ar.al("f", 1, 512)
    oc = ar.al("f", 64, 512)
    xts = ar.al("f", 128, 512)
    e1 = ar.al("f", 128, 512)
    e2 = ar.al("f", 128, 512)
    e3 = ar.al("f", 128, 512)
    e4 = ar.al("f", 128, 512)
    wof = [ar.al("f", 128, 128) for _ in range(2)]
    scz_sb = ar.al("f", 128, 128)
    acat = ar.al("b", 128, 4 * T)
    vaug = ar.al("b", 128, 128 * 65)
    kh = [ar.al("b", 64, 2048) for _ in range(2)]
    rsb = ar.al("b", 128, T)
    qs = ar.al("b", 64, 512)
    pT = [ar.al("b", 128, 512) for _ in range(2)]
    qbf = [ar.al("b", 128, 128) for _ in range(2)]
    wobf = [ar.al("b", 128, 128) for _ in range(2)]
    qwt = [ar.al("i", 128, 128) for _ in range(2)]
    qbi = [ar.al("i", 128, 128) for _ in range(2)]
    sh_sb = ar.al("i", 128, 16)
    pS = [pS2a, pS2b]
    pO = pP[:, 0:512]
    pF = pP[:, 0:1024]
    pW = pP[:, 1024:2048]

    p.step("sync", lambda e: e.dma_start(out=sh_sb[:, 0:1], in_=shifts), inc="ld", amt=16)
    p.step("sync", lambda e: e.dma_start(out=scz_sb[:, 0:128], in_=scz), inc="ld", amt=16)

    for tc in range(4):
        for ic in range(DT):
            p.step("tensor", lambda e, tc=tc, ic=ic: e.matmul(
                pS[0][0:1, :], ones_b[:], big[:, ic * T + tc * 512: ic * T + tc * 512 + 512],
                start=(ic == 0), stop=(ic == DT - 1)), inc="mm")
        mdone = p.cnt["mm"]
        p.wait("vector", "mm", mdone)
        v = p.step("vector", lambda e, tc=tc: e.tensor_copy(rs_row[:, tc * 512:(tc + 1) * 512], pS[0][0:1, :]), inc="dve")
        p.wait("tensor", "dve", v)
    for tc in range(4):
        m = p.step("tensor", lambda e, tc=tc: e.matmul(pO[:], ones_f[:], rs_row[0:1, tc * 512:(tc + 1) * 512], start=True, stop=True), inc="mm")
        p.wait("scalar", "mm", m)
        a = p.step("scalar", lambda e, tc=tc: e.activation(rsb[:, tc * 512:(tc + 1) * 512], pO[:], AF.Copy), inc="act")
        p.wait("tensor", "act", a)

    p.wait("sync", "cc", 2)
    kguard = {}
    for h in range(8):
        p.wait("gpsimd", "mm")
        gv = p.step("gpsimd", lambda e: e.memset(vaug[:], 1.0), inc="gp")
        p.wait("sync", "gp", gv)
        for kg in range(8):
            vld = p.step("sync", lambda e, h=h, kg=kg: e.dma_start(
                out=vaug.rearrange("p (c w) -> p c w", w=65)[:, kg * 16:(kg + 1) * 16, 0:64],
                in_=v_all.rearrange("(c p) d -> p c d", p=128)[:, kg * 16:(kg + 1) * 16, h * 64:(h + 1) * 64]), inc="ld", amt=16)
        vaug_ready = p.cnt["ld"]
        for qg in range(4):
            qld = p.step("sync", lambda e, h=h, qg=qg: e.dma_start(out=qs[:], in_=qT_i[h * 64:(h + 1) * 64, qg * 512:(qg + 1) * 512]), inc="ld", amt=16)
            p.wait("tensor", "ld", qld)
            p.wait("tensor", "ld", vaug_ready)
            for piece in range(8):
                par = piece % 2
                if par in kguard:
                    p.wait("sync", "mm", kguard[par])
                kld = p.step("sync", lambda e, h=h, piece=piece, par=par: e.dma_start(
                    out=kh[par][:], in_=kT_all[piece * 512 + h * 64: piece * 512 + (h + 1) * 64, :]), inc="ld", amt=16)
                p.wait("tensor", "ld", kld)
                for kc in range(16):
                    ppar = kc % 2
                    mS = p.step("tensor", lambda e, kc=kc, par=par, ppar=ppar: e.matmul(
                        pS[ppar][:], kh[par][:, kc * 128:(kc + 1) * 128], qs[:], start=True, stop=True), inc="mm")
                    p.wait("scalar", "mm", mS)
                    aE = p.step("scalar", lambda e, ppar=ppar: e.activation(pT[ppar][:], pS[ppar][:], AF.Exp, bias=np.float32(-CSHIFT)), inc="act")
                    p.wait("tensor", "act", aE)
                    kabs = piece * 16 + kc
                    p.step("tensor", lambda e, kabs=kabs, ppar=ppar: e.matmul(
                        pO[0:65, :], vaug[:, kabs * 65: kabs * 65 + 65], pT[ppar][:],
                        start=(kabs == 0), stop=(kabs == 127)), inc="mm")
                kguard[par] = p.cnt["mm"]
            av_done = p.cnt["mm"]
            p.wait("vector", "mm", av_done)
            p.chain("vector", lambda e: e.tensor_copy(l0[:], pO[64:65, :]), inc="dve")
            p.chain("vector", lambda e: e.tensor_scalar(l1[:], l0[:], SQRT512, None, OP.mult), inc="dve")
            v3 = p.chain("vector", lambda e: e.reciprocal(l0[:], l1[:]), inc="dve")
            p.wait("tensor", "dve", v3)
            mB = p.step("tensor", lambda e: e.matmul(pS[0][0:64, :], ones_f[0:1, 0:64], l0[:], start=True, stop=True), inc="mm")
            p.wait("scalar", "mm", av_done)
            aO = p.step("scalar", lambda e: e.activation(oc[:], pO[0:64, :], AF.Copy), inc="act")
            p.wait("vector", "mm", mB)
            p.wait("vector", "act", aO)
            vz = p.step("vector", lambda e, h=h, qg=qg: e.tensor_tensor(
                acat[(h % 2) * 64:(h % 2) * 64 + 64, (h // 2) * T + qg * 512:(h // 2) * T + qg * 512 + 512],
                oc[:], pS[0][0:64, :], OP.mult), inc="dve")
            p.wait("tensor", "act", aO)
            p.wait("tensor", "dve", vz)
            p.wait("scalar", "dve", vz)
            p.wait("sync", "mm", av_done)

    qguard = {}
    woguard = {}
    for ot in range(DT):
        wo_cv = []
        for hc in range(4):
            par = hc % 2
            wld = p.step("sync", lambda e, ot=ot, hc=hc, par=par: e.dma_start(out=wof[par][:], in_=woT0[hc * 128:(hc + 1) * 128, ot * 128:(ot + 1) * 128]), inc="ld", amt=16)
            p.wait("scalar", "ld", wld)
            if par in woguard:
                p.wait("scalar", "mm", woguard[par])
            cv = p.step("scalar", lambda e, par=par: e.copy(wobf[par][:], wof[par][:]), inc="act")
            wo_cv.append(cv)
            p.wait("sync", "act", cv)
        for tc in range(4):
            fpar = tc % 2
            for ic in range(DT):
                qpar = ic % 2
                if qpar in qguard:
                    p.wait("vector", "mm", qguard[qpar])
                qld = p.step("sync", lambda e, ot=ot, ic=ic, qpar=qpar: e.dma_start(out=qwt[qpar][:], in_=qwr[ic * 128:(ic + 1) * 128, ot * 128:(ot + 1) * 128]), inc="ld", amt=16)
                p.wait("vector", "ld", qld)
                u1 = p.chain("vector", lambda e, qpar=qpar: e.tensor_scalar(qbi[qpar][:], qwt[qpar][:], sh_sb[:, 0:1], 15, OP.logical_shift_right, OP.bitwise_and), inc="dve")
                u2 = p.step("vector", lambda e, qpar=qpar: e.tensor_copy(qbf[qpar][:], qbi[qpar][:]), inc="dve")
                p.wait("tensor", "dve", u2)
                p.step("tensor", lambda e, ic=ic, tc=tc, fpar=fpar, qpar=qpar: e.matmul(
                    pF[:, fpar * 512:(fpar + 1) * 512], qbf[qpar][:],
                    big[:, ic * T + tc * 512: ic * T + tc * 512 + 512], start=(ic == 0), stop=(ic == DT - 1)), inc="mm")
                qguard[qpar] = p.cnt["mm"]
                p.wait("sync", "dve", u1)
            ffn_done = p.cnt["mm"]
            for hc in range(4):
                p.wait("tensor", "act", wo_cv[hc])
                p.step("tensor", lambda e, hc=hc, tc=tc, fpar=fpar: e.matmul(
                    pW[:, fpar * 512:(fpar + 1) * 512], wobf[hc % 2][:],
                    acat[:, hc * T + tc * 512: hc * T + tc * 512 + 512], start=(hc == 0), stop=(hc == 3)), inc="mm")
            wo_done = p.cnt["mm"]
            p.wait("vector", "mm", ffn_done)
            w1 = p.chain("vector", lambda e, ot=ot, fpar=fpar: e.tensor_scalar(
                e1[:], pF[:, fpar * 512:(fpar + 1) * 512], scz_sb[:, ot:ot + 1], scz_sb[:, 2 * DT + ot:2 * DT + ot + 1], OP.mult, OP.add), inc="dve")
            p.chain("vector", lambda e, ot=ot, tc=tc: e.tensor_scalar(
                e2[:], rsb[:, tc * 512:(tc + 1) * 512], scz_sb[:, DT + ot:DT + ot + 1], None, OP.mult), inc="dve")
            p.chain("vector", lambda e: e.tensor_tensor(e3[:], e1[:], e2[:], OP.subtract), inc="dve")
            p.wait("scalar", "mm", wo_done)
            p.wait("scalar", "dve", p.cnt["dve"])
            w4 = p.step("scalar", lambda e, ot=ot, fpar=fpar: e.activation(
                e4[:], pW[:, fpar * 512:(fpar + 1) * 512], AF.Identity, bias=scz_sb[:, 3 * DT + ot:3 * DT + ot + 1]), inc="act")
            xld = p.step("sync", lambda e, ot=ot, tc=tc: e.dma_start(out=xts[:], in_=xT_i[ot * 128:(ot + 1) * 128, tc * 512:(tc + 1) * 512]), inc="ld", amt=16)
            p.wait("vector", "act", w4)
            p.chain("vector", lambda e: e.tensor_tensor(e2[:], e3[:], e4[:], OP.add), inc="dve")
            p.wait("vector", "ld", xld)
            w6 = p.chain("vector", lambda e: e.tensor_tensor(e1[:], e2[:], xts[:], OP.add), inc="dve")
            p.wait("gpsimd", "dve", w6)
            p.step("gpsimd", lambda e, ot=ot, tc=tc: e.dma_start(out=x1T_i[ot * 128:(ot + 1) * 128, tc * 512:(tc + 1) * 512], in_=e1[:]), inc="st", amt=16)
            p.wait("tensor", "dve", w1)
            p.wait("tensor", "act", w4)
            p.wait("vector", "st")
        woguard[0] = p.cnt["mm"]
        woguard[1] = p.cnt["mm"]
        p.wait("sync", "dve")

    p.barrier()
    ar.reset()

    x1c = ar.al("f", 128, 1024)
    srow = ar.al("f", 1, 1536)
    wqz = [ar.al("f", 128, 128) for _ in range(2)]
    sigb_sb = ar.al("f", 128, 64)
    ocn = ar.al("f", 128, 256)
    wa = ar.al("f", 128, 256)
    wbv = ar.al("f", 128, 256)
    bl = ar.al("f", 128, 256)
    blt = ar.al("f", 128, 256)
    wkvz = ar.al("f", 128, 1024)
    haccF = ar.al("f", 128, 256)
    htmpF = ar.al("f", 128, 256)
    hm_sb = ar.al("f", 128, 16)
    hi5 = ar.al("b", 128, 512)
    lo5 = ar.al("b", 128, 512)
    sq5 = ar.al("b", 128, 512)
    mb = ar.al("b", 128, T)
    rb = ar.al("b", 128, T)
    xn2c = ar.al("b", 128, 512)
    wkvb = ar.al("b", 128, DT * 128)
    cosS = ar.al("b", 128, T)
    sinS = ar.al("b", 128, T)
    t1 = ar.al("b", 128, 1024)
    t2 = ar.al("b", 128, 1024)
    kr = ar.al("b", 128, HW)
    vstage = ar.al("b", 128, 128)
    vtk = ar.al("b", 128, HW)
    wqb = [ar.al("b", 128, 128) for _ in range(2)]
    qr = ar.al("b", 128, T)
    pTt = ar.al("b", 128, 256)
    abuf = ar.al("b", 128, 256)
    alphJ = ar.al("b", 128, 256)
    cand = [ar.al("b", 128, 256) for _ in range(2)]
    l0c = srow[0:1, 0:256]
    pQ = pP
    pS_ = pS2a
    pOc = pS2b
    pBw = pP[:, 0:256]
    pTr = pXb1[:, 0:128]

    p.step("sync", lambda e: e.dma_start(out=cosS[:], in_=cosK), inc="ld", amt=16)
    p.step("sync", lambda e: e.dma_start(out=sinS[:], in_=sinKs), inc="ld", amt=16)
    p.step("sync", lambda e: e.dma_start(out=sigb_sb[:], in_=sigb), inc="ld", amt=16)
    p.step("sync", lambda e: e.dma_start(out=hm_sb[:], in_=hmask), inc="ld", amt=16)

    for th in range(4):
        for dc in range(DT):
            xld = p.step("sync", lambda e, dc=dc, th=th: e.dma_start(
                out=x1c[:, 0:512], in_=x1T_i[dc * 128:(dc + 1) * 128, th * 512:(th + 1) * 512]), inc="ld", amt=16)
            p.wait("vector", "ld", xld)
            p.chain("vector", lambda e: e.tensor_copy(hi5[:], x1c[:, 0:512]), inc="dve")
            vl = p.chain("vector", lambda e: e.tensor_tensor(lo5[:], x1c[:, 0:512], hi5[:], OP.subtract), inc="dve")
            p.wait("scalar", "ld", xld)
            sa = p.step("scalar", lambda e: e.activation(sq5[:], x1c[:, 0:512], AF.Square), inc="act")
            p.wait("tensor", "dve", vl)
            p.wait("tensor", "act", sa)
            p.step("tensor", lambda e, dc=dc: e.matmul(pS_[0:1, :], ones_b[:], hi5[:], start=(dc == 0), stop=False), inc="mm")
            p.step("tensor", lambda e, dc=dc: e.matmul(pS_[0:1, :], ones_b[:], lo5[:], start=False, stop=(dc == DT - 1)), inc="mm")
            p.step("tensor", lambda e, dc=dc: e.matmul(pOc[0:1, :], ones_b[:], sq5[:], start=(dc == 0), stop=(dc == DT - 1)), inc="mm")
            mm_now = p.cnt["mm"]
            p.wait("vector", "mm", mm_now)
            p.wait("scalar", "mm", mm_now)
            p.wait("sync", "dve", vl)
            p.wait("sync", "act", sa)
        mdone = p.cnt["mm"]
        p.wait("vector", "mm", mdone)
        p.chain("vector", lambda e: e.tensor_copy(srow[:, 0:512], pS_[0:1, :]), inc="dve")
        p.chain("vector", lambda e: e.tensor_copy(srow[:, 512:1024], pOc[0:1, :]), inc="dve")
        p.chain("vector", lambda e: e.tensor_scalar(srow[:, 0:512], srow[:, 0:512], 1.0 / D, None, OP.mult), inc="dve")
        p.chain("vector", lambda e: e.tensor_tensor(srow[:, 1024:1536], srow[:, 0:512], srow[:, 0:512], OP.mult), inc="dve")
        p.chain("vector", lambda e: e.tensor_scalar(srow[:, 512:1024], srow[:, 512:1024], 1.0 / D, None, OP.mult), inc="dve")
        v6 = p.chain("vector", lambda e: e.tensor_tensor(srow[:, 1024:1536], srow[:, 512:1024], srow[:, 1024:1536], OP.subtract), inc="dve")
        p.wait("scalar", "dve", v6)
        a1 = p.step("scalar", lambda e: e.activation(srow[:, 1024:1536], srow[:, 1024:1536], AF.Sqrt, bias=np.float32(EPS)), inc="act")
        p.wait("vector", "act", a1)
        v7 = p.chain("vector", lambda e: e.reciprocal(srow[:, 1024:1536], srow[:, 1024:1536]), inc="dve")
        p.wait("tensor", "dve", v7)
        for half in range(2):
            m1 = p.step("tensor", lambda e, half=half: e.matmul(pS_[:, 0:256], ones_f[:], srow[0:1, half * 256:half * 256 + 256], start=True, stop=True), inc="mm")
            m2 = p.step("tensor", lambda e, half=half: e.matmul(pOc[:, 0:256], ones_f[:], srow[0:1, 1024 + half * 256:1024 + half * 256 + 256], start=True, stop=True), inc="mm")
            p.wait("scalar", "mm", m2)
            a2 = p.step("scalar", lambda e, th=th, half=half: e.activation(mb[:, th * 512 + half * 256: th * 512 + half * 256 + 256], pS_[:, 0:256], AF.Copy), inc="act")
            a3 = p.step("scalar", lambda e, th=th, half=half: e.activation(rb[:, th * 512 + half * 256: th * 512 + half * 256 + 256], pOc[:, 0:256], AF.Copy), inc="act")
            p.wait("tensor", "act", a3)

    for dc in range(DT):
        for th in range(4):
            off = th * 512
            xld = p.step("sync", lambda e, dc=dc, off=off: e.dma_start(
                out=x1c[:, 0:512], in_=x1T_i[dc * 128:(dc + 1) * 128, off:off + 512]), inc="ld", amt=16)
            p.wait("vector", "ld", xld)
            p.chain("vector", lambda e, off=off: e.tensor_tensor(xn2c[:, 0:512], x1c[:, 0:512], mb[:, off:off + 512], OP.subtract), inc="dve")
            n2 = p.chain("vector", lambda e, dc=dc, off=off: e.tensor_tensor(
                big[:, dc * T + off: dc * T + off + 512], xn2c[:, 0:512], rb[:, off:off + 512], OP.mult), inc="dve")
            p.wait("sync", "dve", n2)
        p.wait("gpsimd", "dve", p.cnt["dve"])
        p.step("gpsimd", lambda e, dc=dc: e.dma_start(out=xn2T_i[dc * 128:(dc + 1) * 128, :], in_=big[:, dc * T:(dc + 1) * T]), inc="st", amt=16)
    p.wait("tensor", "dve", p.cnt["dve"])

    kvguard = 0
    for kv in range(2):
        for g in range(4):
            lw = p.step("sync", lambda e, g=g, kv=kv: e.dma_start(
                out=wkvz[:].rearrange("p (c o) -> p c o", o=128),
                in_=wkvT1.rearrange("(c p) o -> p c o", p=128)[:, g * 8:(g + 1) * 8, kv * 128:(kv + 1) * 128]), inc="ld", amt=16)
            p.wait("scalar", "ld", lw)
            if kv == 1:
                p.wait("scalar", "mm", kvguard)
            cw = p.step("scalar", lambda e, g=g: e.copy(wkvb[:, g * 1024:(g + 1) * 1024], wkvz[:]), inc="act")
            p.wait("sync", "act", cw)
        p.wait("tensor", "act")
        for dc in range(DT):
            for cc in range(4):
                p.step("tensor", lambda e, dc=dc, cc=cc: e.matmul(
                    pQ[:, cc * 512:(cc + 1) * 512], wkvb[:, dc * 128:(dc + 1) * 128],
                    big[:, dc * T + cc * 512: dc * T + (cc + 1) * 512],
                    start=(dc == 0), stop=(dc == DT - 1)), inc="mm")
        mdone = p.cnt["mm"]
        kvguard = mdone
        p.wait("vector", "mm", mdone)
        p.wait("scalar", "mm", mdone)
        if kv == 0:
            for hf in range(2):
                o1 = hf * 1024
                p.chain("vector", lambda e, o1=o1: e.tensor_tensor(t1[:], pQ[:, o1:o1 + 1024], cosS[:, o1:o1 + 1024], OP.mult), inc="dve")
                p.chain("vector", lambda e, o1=o1: e.tensor_tensor(t2[0:64, :], pQ[64:128, o1:o1 + 1024], sinS[0:64, o1:o1 + 1024], OP.mult), inc="dve")
                p.chain("vector", lambda e, o1=o1: e.tensor_tensor(t2[64:128, :], pQ[0:64, o1:o1 + 1024], sinS[64:128, o1:o1 + 1024], OP.mult), inc="dve")
                vk = p.chain("vector", lambda e, o1=o1: e.tensor_tensor(kr[:, 256 + o1:256 + o1 + 1024], t1[:], t2[:], OP.add), inc="dve")
            p.wait("tensor", "dve", vk)
        else:
            for ch in range(16):
                src = pQ[:, ch * 128:(ch + 1) * 128]
                av = p.step("scalar", lambda e, src=src: e.activation(vstage[:], src, AF.Copy), inc="act")
                p.wait("tensor", "act", av)
                mt = p.step("tensor", lambda e: e.transpose(pTr, vstage[:], ident[:]), inc="mm")
                p.wait("scalar", "mm", mt)
                at = p.step("scalar", lambda e, ch=ch: e.activation(vtk[:, 256 + ch * 128:256 + (ch + 1) * 128], pTr, AF.Copy), inc="act")
                p.wait("tensor", "act", at)

    p.wait("gpsimd", "dve")
    p.wait("gpsimd", "act")
    p.step("gpsimd", lambda e: e.dma_start(out=kv_loc[0:128, :], in_=kr[:, 256:256 + T]), inc="st", amt=16)
    p.step("gpsimd", lambda e: e.dma_start(out=kv_loc[128:256, :], in_=vtk[:, 256:256 + T]), inc="st", amt=16)
    p.wait("gpsimd", "st")
    p.step("gpsimd", lambda e: e.collective_compute(
        "AllGather", mybir.AluOpType.bypass, replica_groups=RG,
        ins=[kv_loc.opt()], outs=[kv_all.opt()]), inc="cc")
    p.wait("sync", "cc", 3)
    for side in range(2):
        srccol = 1792 if side == 0 else 0
        dstcol = 0 if side == 0 else 2304
        for tk in range(2):
            rowoff = 0 if tk == 0 else 128
            target = kr if tk == 0 else vtk
            for r in range(NC):
                cld = p.step("sync", lambda e, r=r, rowoff=rowoff, srccol=srccol: e.dma_start(
                    out=cand[r % 2][:], in_=kv_all[r * 256 + rowoff: r * 256 + rowoff + 128, srccol:srccol + 256]), inc="ld", amt=16)
                p.wait("vector", "ld", cld)
                if r == 0:
                    hv = p.chain("vector", lambda e, side=side, r=r: e.tensor_scalar(
                        haccF[:], cand[r % 2][:], hm_sb[:, side * 8 + r:side * 8 + r + 1], None, OP.mult), inc="dve")
                else:
                    p.chain("vector", lambda e, side=side, r=r: e.tensor_scalar(
                        htmpF[:], cand[r % 2][:], hm_sb[:, side * 8 + r:side * 8 + r + 1], None, OP.mult), inc="dve")
                    hv = p.chain("vector", lambda e: e.tensor_tensor(haccF[:], haccF[:], htmpF[:], OP.add), inc="dve")
                p.wait("sync", "dve", hv)
            hc = p.chain("vector", lambda e, target=target, dstcol=dstcol: e.tensor_copy(target[:, dstcol:dstcol + 256], haccF[:]), inc="dve")
    p.wait("tensor", "dve", hc)

    qg_guard = {}
    for h in range(32):
        for dc in range(DT):
            par = dc % 2
            wld = p.step("sync", lambda e, h=h, dc=dc, par=par: e.dma_start(
                out=wqz[par][:], in_=wqT1[dc * 128:(dc + 1) * 128, h * 128:(h + 1) * 128]), inc="ld", amt=16)
            p.wait("scalar", "ld", wld)
            if par in qg_guard:
                p.wait("scalar", "mm", qg_guard[par])
            cv = p.step("scalar", lambda e, par=par: e.copy(wqb[par][:], wqz[par][:]), inc="act")
            p.wait("tensor", "act", cv)
            for cc in range(4):
                p.step("tensor", lambda e, dc=dc, cc=cc, par=par: e.matmul(
                    pQ[:, cc * 512:(cc + 1) * 512], wqb[par][:], big[:, dc * T + cc * 512: dc * T + (cc + 1) * 512],
                    start=(dc == 0), stop=(dc == DT - 1)), inc="mm")
            qg_guard[par] = p.cnt["mm"]
            p.wait("sync", "act", cv)
        qdone = p.cnt["mm"]
        p.wait("vector", "mm", qdone)
        for hf in range(2):
            o1 = hf * 1024
            p.chain("vector", lambda e, o1=o1: e.tensor_tensor(t1[:], pQ[:, o1:o1 + 1024], cosS[:, o1:o1 + 1024], OP.mult), inc="dve")
            p.chain("vector", lambda e, o1=o1: e.tensor_tensor(t2[0:64, :], pQ[64:128, o1:o1 + 1024], sinS[0:64, o1:o1 + 1024], OP.mult), inc="dve")
            p.chain("vector", lambda e, o1=o1: e.tensor_tensor(t2[64:128, :], pQ[0:64, o1:o1 + 1024], sinS[64:128, o1:o1 + 1024], OP.mult), inc="dve")
            r4 = p.chain("vector", lambda e, o1=o1: e.tensor_tensor(qr[:, o1:o1 + 1024], t1[:], t2[:], OP.add), inc="dve")
        p.wait("tensor", "dve", r4)
        for b in range(8):
            ald = p.step("sync", lambda e, b=b: e.dma_start(out=alphJ[:], in_=alph[:, b * 256:(b + 1) * 256]), inc="ld", amt=16)
            for w in range(2):
                k0 = (b + w) * 256
                for kc in range(4):
                    mS = p.step("tensor", lambda e, k0=k0, kc=kc, b=b: e.matmul(
                        pS_[:, 0:256], kr[:, k0 + kc * 128:k0 + (kc + 1) * 128], qr[:, b * 256:(b + 1) * 256],
                        start=True, stop=True), inc="mm")
                    idx = b * 8 + w * 4 + kc
                    p.wait("scalar", "mm", mS)
                    aS = p.step("scalar", lambda e, idx=idx: e.activation(pTt[:], pS_[:, 0:256], AF.Sigmoid, bias=sigb_sb[:, idx:idx + 1], scale=np.float32(1.0 / np.sqrt(128.0))), inc="act")
                    p.wait("tensor", "act", aS)
                    p.step("tensor", lambda e, kc=kc: e.matmul(pBw[0:1, :], ones_b[:], pTt[:], start=(kc == 0), stop=(kc == 3)), inc="mm")
                    mA = p.step("tensor", lambda e, k0=k0, kc=kc: e.matmul(
                        pOc[:, 0:256], vtk[:, (k0 // 128 + kc) * 128:(k0 // 128 + kc + 1) * 128], pTt[:],
                        start=(kc == 0), stop=(kc == 3)), inc="mm")
                    p.wait("scalar", "mm", mA)
                wdone = p.cnt["mm"]
                p.wait("vector", "mm", wdone)
                v1 = p.chain("vector", lambda e: e.reciprocal(l0c[:], pBw[0:1, :]), inc="dve")
                p.wait("tensor", "dve", v1)
                mB = p.step("tensor", lambda e: e.matmul(pBw[:], ones_f[:], l0c[:], start=True, stop=True), inc="mm")
                p.wait("scalar", "mm", wdone)
                aO = p.step("scalar", lambda e: e.activation(ocn[:], pOc[:, 0:256], AF.Copy), inc="act")
                p.wait("vector", "act", aO)
                p.wait("vector", "mm", mB)
                dst = wa if w == 0 else wbv
                vN = p.step("vector", lambda e, dst=dst: e.tensor_tensor(dst[:], ocn[:], pBw[:], OP.mult), inc="dve")
                p.wait("tensor", "dve", vN)
                p.wait("scalar", "dve", vN)
            p.wait("vector", "ld", ald)
            p.chain("vector", lambda e: e.tensor_scalar(blt[:], alphJ[:], -1.0, 1.0, OP.mult, OP.add), inc="dve")
            p.chain("vector", lambda e: e.tensor_tensor(bl[:], wa[:], blt[:], OP.mult), inc="dve")
            p.chain("vector", lambda e: e.tensor_tensor(blt[:], wbv[:], alphJ[:], OP.mult), inc="dve")
            p.chain("vector", lambda e: e.tensor_tensor(bl[:], bl[:], blt[:], OP.add), inc="dve")
            b4 = p.chain("vector", lambda e: e.tensor_copy(abuf[:], bl[:]), inc="dve")
            p.wait("sync", "dve", b4)
            p.wait("gpsimd", "dve", b4)
            gst = p.step("gpsimd", lambda e, h=h, b=b: e.dma_start(out=acT_i[h * 128:(h + 1) * 128, b * 256:(b + 1) * 256], in_=abuf[:]), inc="st", amt=16)
            p.wait("vector", "st", gst)

    p.barrier()
    ar.reset()

    wzD = [ar.al("f", 128, 128) for _ in range(2)]
    xtsD = ar.al("f", 128, T)
    evD = ar.al("f", 128, 512)
    wbD = [ar.al("b", 128, 128) for _ in range(2)]
    pWD = pP

    for dc in range(DT):
        p.step("sync", lambda e, dc=dc: e.dma_start(out=big[:, dc * T:(dc + 1) * T], in_=acT_i[dc * 128:(dc + 1) * 128, :]), inc="ld", amt=16)
    acld = p.cnt["ld"]
    p.wait("tensor", "ld", acld)
    guardD = {}
    for ot in range(DT):
        xld = p.step("sync", lambda e, ot=ot: e.dma_start(out=xtsD[:], in_=x1T_i[ot * 128:(ot + 1) * 128, :]), inc="ld", amt=16)
        for dc in range(DT):
            par = dc % 2
            wld = p.step("sync", lambda e, ot=ot, dc=dc, par=par: e.dma_start(out=wzD[par][:], in_=woT1[dc * 128:(dc + 1) * 128, ot * 128:(ot + 1) * 128]), inc="ld", amt=16)
            p.wait("scalar", "ld", wld)
            if par in guardD:
                p.wait("scalar", "mm", guardD[par])
            cv = p.step("scalar", lambda e, par=par: e.copy(wbD[par][:], wzD[par][:]), inc="act")
            p.wait("tensor", "act", cv)
            for cc in range(4):
                p.step("tensor", lambda e, dc=dc, cc=cc, par=par: e.matmul(
                    pWD[:, cc * 512:(cc + 1) * 512], wbD[par][:], big[:, dc * T + cc * 512: dc * T + (cc + 1) * 512],
                    start=(dc == 0), stop=(dc == DT - 1)), inc="mm")
            guardD[par] = p.cnt["mm"]
            p.wait("sync", "act", cv)
        mdone = p.cnt["mm"]
        p.wait("vector", "mm", mdone)
        p.wait("vector", "ld", xld)
        for cc in range(4):
            vv = p.chain("vector", lambda e, cc=cc: e.tensor_tensor(evD[:], pWD[:, cc * 512:(cc + 1) * 512], xtsD[:, cc * 512:(cc + 1) * 512], OP.add), inc="dve")
            p.wait("gpsimd", "dve", vv)
            gst = p.step("gpsimd", lambda e, ot=ot, cc=cc: e.dma_start(out=pt_i[ot * 128:(ot + 1) * 128, cc * 512:(cc + 1) * 512], in_=evD[:]), inc="st", amt=16)
            p.wait("vector", "st", gst)
        p.wait("tensor", "dve")
        p.wait("sync", "dve")

    p.barrier()
    ar.reset()

    wgz = ar.al("f", 128, 256)
    wtz = ar.al("f", 128, 32)
    aaz = ar.al("f", 128, 128)
    btr_sb = ar.al("f", 1, 16)
    bco_sb = ar.al("f", 128, 32)
    gate_t = ar.al("f", 128, 8)
    ge = ar.al("f", 128, 8)
    gmx = ar.al("f", 128, 6)
    thr_t = ar.al("f", 128, 1)
    wtok = ar.al("f", 128, 8)
    wzE = [ar.al("f", 128, 128) for _ in range(2)]
    pts = ar.al("f", 128, T)
    x2c = ar.al("f", 128, T)
    m1s = ar.al("f", 128, 512)
    tt1 = ar.al("f", 128, 128)
    tt2 = ar.al("f", 128, 128)
    tt3 = ar.al("f", 128, 128)
    wgb = ar.al("b", 128, 256)
    wtb = ar.al("b", 128, 32)
    aab = ar.al("b", 128, DT * 128)
    bab = ar.al("b", 128, DT * 128)
    eeb = ar.al("b", 8, 128)
    g8 = ar.al("b", 8, T)
    thr_r = ar.al("b", 1, T)
    wtkb = ar.al("b", 128, 8)
    wT8 = ar.al("b", 8, T)
    we = ar.al("b", 128, T)
    hT = ar.al("b", 128, T)
    gsb = ar.al("b", 128, T)
    hiE = ar.al("b", 128, T)
    loE = ar.al("b", 128, T)
    wbE = [ar.al("b", 128, 128) for _ in range(2)]
    pG = pS2a
    pH = pS2b
    pM = pP[:, 0:1024]
    pB_ = pP[:, 1024:2048]
    pt1 = pXb1[:, 0:128]
    pt2 = pXb2[:, 0:128]

    p.step("sync", lambda e: e.dma_start(out=btr_sb[0:1, 0:1], in_=btr), inc="ld", amt=16)
    p.step("sync", lambda e: e.dma_start(out=bco_sb[:, 0:DT], in_=bco), inc="ld", amt=16)
    p.step("sync", lambda e: e.dma_start(out=eeb[:], in_=eex), inc="ld", amt=16)
    p.step("sync", lambda e: e.dma_start(out=wgz[:].rearrange("p (c o) -> p c o", o=8), in_=wgT1.rearrange("(c p) o -> p c o", p=128)), inc="ld", amt=16)
    p.step("sync", lambda e: e.dma_start(out=wtz[:, 0:DT], in_=wtr), inc="ld", amt=16)
    for dc in range(DT):
        p.step("sync", lambda e, dc=dc: e.dma_start(out=big[:, dc * T:(dc + 1) * T], in_=xn2T_i[dc * 128:(dc + 1) * 128, :]), inc="ld", amt=16)
    ld0 = p.cnt["ld"]
    p.wait("scalar", "ld", ld0)
    p.step("scalar", lambda e: e.copy(wgb[:], wgz[:]), inc="act")
    p.step("scalar", lambda e: e.copy(wtb[:, 0:DT], wtz[:, 0:DT]), inc="act")
    for dc in range(DT):
        l1_ = p.step("sync", lambda e, dc=dc: e.dma_start(out=aaz[:], in_=aaT[dc * 128:(dc + 1) * 128, :]), inc="ld", amt=16)
        p.wait("scalar", "ld", l1_)
        ca = p.step("scalar", lambda e, dc=dc: e.copy(aab[:, dc * 128:(dc + 1) * 128], aaz[:]), inc="act")
        p.wait("sync", "act", ca)
        l2_ = p.step("sync", lambda e, dc=dc: e.dma_start(out=aaz[:], in_=baT[:, dc * 128:(dc + 1) * 128]), inc="ld", amt=16)
        p.wait("scalar", "ld", l2_)
        cb = p.step("scalar", lambda e, dc=dc: e.copy(bab[:, dc * 128:(dc + 1) * 128], aaz[:]), inc="act")
        p.wait("sync", "act", cb)
    cdone = p.cnt["act"]
    p.wait("tensor", "act", cdone)

    for tc in range(4):
        for dc in range(DT):
            p.step("tensor", lambda e, dc=dc, tc=tc: e.matmul(
                pG[0:8, :], wgb[:, dc * 8:(dc + 1) * 8], big[:, dc * T + tc * 512: dc * T + (tc + 1) * 512],
                start=(dc == 0), stop=(dc == DT - 1)), inc="mm")
        g_done = p.cnt["mm"]
        for dc in range(DT):
            p.step("tensor", lambda e, dc=dc, tc=tc: e.matmul(
                pH[0:1, :], wtb[:, dc:dc + 1], big[:, dc * T + tc * 512: dc * T + (tc + 1) * 512],
                start=(dc == 0), stop=(dc == DT - 1)), inc="mm")
        t_done = p.cnt["mm"]
        for dc in range(DT):
            p.step("tensor", lambda e, dc=dc, tc=tc: e.matmul(
                pM[:, 0:512], aab[:, dc * 128:(dc + 1) * 128], big[:, dc * T + tc * 512: dc * T + (tc + 1) * 512],
                start=(dc == 0), stop=(dc == DT - 1)), inc="mm")
        h_done = p.cnt["mm"]
        p.wait("scalar", "mm", g_done)
        a1 = p.step("scalar", lambda e, tc=tc: e.activation(g8[0:8, tc * 512:(tc + 1) * 512], pG[0:8, :], AF.Copy), inc="act")
        p.wait("scalar", "mm", t_done)
        a2 = p.step("scalar", lambda e, tc=tc: e.activation(thr_r[:, tc * 512:(tc + 1) * 512], pH[0:1, :], AF.Sigmoid, bias=btr_sb[0:1, 0:1]), inc="act")
        p.wait("scalar", "mm", h_done)
        a3 = p.step("scalar", lambda e, tc=tc: e.activation(hT[:, tc * 512:(tc + 1) * 512], pM[:, 0:512], AF.Copy), inc="act")
        p.wait("tensor", "act", a3)

    for tt in range(16):
        mt = p.step("tensor", lambda e, tt=tt: e.transpose(pt1[0:128, 0:8], g8[0:8, tt * 128:(tt + 1) * 128], ident[0:8, 0:8]), inc="mm")
        mt2 = p.step("tensor", lambda e, tt=tt: e.transpose(pt2[0:128, 0:1], thr_r[0:1, tt * 128:(tt + 1) * 128], ident[0:1, 0:1]), inc="mm")
        p.wait("scalar", "mm", mt2)
        ag1 = p.step("scalar", lambda e: e.activation(gate_t[:], pt1[0:128, 0:8], AF.Copy), inc="act")
        ag2 = p.step("scalar", lambda e: e.activation(thr_t[:], pt2[0:128, 0:1], AF.Copy), inc="act")
        p.wait("vector", "act", ag2)
        p.chain("vector", lambda e: e.reduce_max(gmx[:, 0:1], gate_t[:], axis=AX.X), inc="dve")
        v3 = p.chain("vector", lambda e: e.tensor_scalar(gmx[:, 1:2], gmx[:, 0:1], -1.0, None, OP.mult), inc="dve")
        p.wait("tensor", "dve", v3)
        p.wait("scalar", "dve", v3)
        a1 = p.step("scalar", lambda e: e.activation(ge[:], gate_t[:], AF.Exp, bias=gmx[:, 1:2]), inc="act")
        p.wait("vector", "act", a1)
        p.chain("vector", lambda e: e.reduce_sum(gmx[:, 2:3], ge[:], axis=AX.X), inc="dve")
        p.chain("vector", lambda e: e.tensor_scalar(gmx[:, 3:4], gmx[:, 2:3], thr_t[:], 0.125, OP.mult, OP.mult), inc="dve")
        p.chain("vector", lambda e: e.tensor_scalar(wtok[:], ge[:], gmx[:, 3:4], 0.0, OP.subtract, OP.max), inc="dve")
        p.chain("vector", lambda e: e.reduce_sum(gmx[:, 4:5], wtok[:], axis=AX.X), inc="dve")
        p.chain("vector", lambda e: e.tensor_scalar(gmx[:, 5:6], gmx[:, 4:5], 0.0, None, OP.is_equal), inc="dve")
        p.chain("vector", lambda e: e.tensor_tensor(gmx[:, 4:5], gmx[:, 4:5], gmx[:, 5:6], OP.add), inc="dve")
        p.chain("vector", lambda e: e.reciprocal(gmx[:, 5:6], gmx[:, 4:5]), inc="dve")
        v11 = p.chain("vector", lambda e: e.tensor_scalar(wtkb[:], wtok[:], gmx[:, 5:6], 2.0, OP.mult, OP.mult), inc="dve")
        p.wait("tensor", "dve", v11)
        mt3 = p.step("tensor", lambda e: e.transpose(pt2[0:8, 0:128], wtkb[:], ident[:]), inc="mm")
        p.wait("scalar", "mm", mt3)
        a2 = p.step("scalar", lambda e, tt=tt: e.activation(wT8[0:8, tt * 128:(tt + 1) * 128], pt2[0:8, 0:128], AF.Copy), inc="act")
        p.wait("tensor", "act", a2)
        p.wait("vector", "act", a2)
    for tc in range(4):
        mwe = p.step("tensor", lambda e, tc=tc: e.matmul(pM[:, 0:512], eeb[:], wT8[0:8, tc * 512:(tc + 1) * 512], start=True, stop=True), inc="mm")
        p.wait("scalar", "mm", mwe)
        awe = p.step("scalar", lambda e, tc=tc: e.activation(we[:, tc * 512:(tc + 1) * 512], pM[:, 0:512], AF.Copy), inc="act")
        p.wait("tensor", "act", awe)
    p.wait("vector", "act")
    vg = p.chain("vector", lambda e: e.tensor_tensor(gsb[:], hT[:], we[:], OP.mult), inc="dve")
    p.wait("tensor", "dve", vg)

    guardE = {}
    for ot in range(DT):
        pld = p.step("sync", lambda e, ot=ot: e.dma_start(out=pts[:], in_=pt_i[ot * 128:(ot + 1) * 128, :]), inc="ld", amt=16)
        for dc in range(DT):
            par = dc % 2
            wld = p.step("sync", lambda e, ot=ot, dc=dc, par=par: e.dma_start(out=wzE[par][:], in_=wbT1[dc * 128:(dc + 1) * 128, ot * 128:(ot + 1) * 128]), inc="ld", amt=16)
            p.wait("scalar", "ld", wld)
            if par in guardE:
                p.wait("scalar", "mm", guardE[par])
            cv = p.step("scalar", lambda e, par=par: e.copy(wbE[par][:], wzE[par][:]), inc="act")
            p.wait("tensor", "act", cv)
            for cc in range(2):
                p.step("tensor", lambda e, dc=dc, cc=cc, par=par: e.matmul(
                    pB_[:, cc * 512:(cc + 1) * 512], wbE[par][:], big[:, dc * T + cc * 512: dc * T + (cc + 1) * 512],
                    start=(dc == 0), stop=(dc == DT - 1)), inc="mm")
            for cc in range(2):
                p.step("tensor", lambda e, dc=dc, cc=cc, par=par: e.matmul(
                    pM[:, cc * 512:(cc + 1) * 512], wbE[par][:], big[:, dc * T + (2 + cc) * 512: dc * T + (3 + cc) * 512],
                    start=(dc == 0), stop=(dc == DT - 1)), inc="mm")
            guardE[par] = p.cnt["mm"]
            p.wait("sync", "act", cv)
        p.wait("vector", "ld", pld)
        for cc in range(4):
            pmo = pG if cc % 2 == 0 else pH
            pbase = pB_ if cc < 2 else pM
            off = (cc % 2) * 512
            mmo = p.step("tensor", lambda e, ot=ot, cc=cc, pmo=pmo: e.matmul(
                pmo[:], bab[:, ot * 128:(ot + 1) * 128], gsb[:, cc * 512:(cc + 1) * 512], start=True, stop=True), inc="mm")
            p.wait("vector", "mm", mmo)
            p.chain("vector", lambda e, ot=ot, pbase=pbase, off=off: e.tensor_scalar(m1s[:], pbase[:, off:off + 512], bco_sb[:, ot:ot + 1], None, OP.add), inc="dve")
            p.chain("vector", lambda e, pmo=pmo: e.tensor_tensor(m1s[:], m1s[:], pmo[:], OP.add), inc="dve")
            z3 = p.chain("vector", lambda e, cc=cc: e.tensor_tensor(x2c[:, cc * 512:(cc + 1) * 512], m1s[:], pts[:, cc * 512:(cc + 1) * 512], OP.add), inc="dve")
            p.wait("tensor", "dve", z3)
        xdone = p.cnt["dve"]
        p.wait("tensor", "dve", xdone)
        p.chain("vector", lambda e: e.tensor_copy(hiE[:], x2c[:]), inc="dve")
        vl = p.chain("vector", lambda e: e.tensor_tensor(loE[:], x2c[:], hiE[:], OP.subtract), inc="dve")
        p.wait("tensor", "dve", vl)
        for tcc in range(16):
            m1 = p.step("tensor", lambda e, tcc=tcc: e.transpose(pt1, hiE[:, tcc * 128:(tcc + 1) * 128], ident[:]), inc="mm")
            m2 = p.step("tensor", lambda e, tcc=tcc: e.transpose(pt2, loE[:, tcc * 128:(tcc + 1) * 128], ident[:]), inc="mm")
            p.wait("scalar", "mm", m1)
            ac1 = p.step("scalar", lambda e: e.activation(tt1[:], pt1, AF.Copy), inc="act")
            p.wait("scalar", "mm", m2)
            ac2 = p.step("scalar", lambda e: e.activation(tt3[:], pt2, AF.Copy), inc="act")
            p.wait("vector", "act", ac2)
            vv = p.step("vector", lambda e: e.tensor_tensor(tt2[:], tt1[:], tt3[:], OP.add), inc="dve")
            p.wait("gpsimd", "dve", vv)
            gst = p.step("gpsimd", lambda e, ot=ot, tcc=tcc: e.dma_start(out=out_o[tcc * 128:(tcc + 1) * 128, ot * 128:(ot + 1) * 128], in_=tt2[:]), inc="st", amt=16)
            p.wait("tensor", "dve", vv)
            p.wait("tensor", "act", ac2)
            p.wait("vector", "st", gst)
            p.wait("scalar", "st", gst)
        p.wait("sync", "dve")
        p.wait("vector", "mm")
    p.barrier(sems=("ld", "st", "mm", "dve", "act", "gp", "cc"))
    p.finalize()
    return nc, stack


def _cols(v):
    v = np.asarray(v, dtype=np.float32).reshape(-1)
    return np.ascontiguousarray(v.reshape(-1, 128).T)


def kernel(**inp):
    x = np.ascontiguousarray(np.asarray(inp["L0_attn_x"], dtype=np.float32).reshape(S, D))

    he = np.arange(512)
    permq = (he // 64) * 128 + 2 * (he % 64)
    perm = np.concatenate([permq, permq + 1])
    wqk = np.asarray(inp["L0_attn_Wqk_w"], dtype=np.float32)[perm]
    wv0 = np.asarray(inp["L0_attn_Wv_w"], dtype=np.float32)
    wqkvT = np.ascontiguousarray(np.concatenate([wqk, wv0], axis=0).T)
    bias12 = np.ascontiguousarray(np.concatenate(
        [_cols(np.asarray(inp["L0_attn_Wqk_b"], dtype=np.float32)[perm]), _cols(inp["L0_attn_Wv_b"])], axis=1))
    woT0 = np.ascontiguousarray(np.asarray(inp["L0_attn_Wo_w"], dtype=np.float32).T)
    qwr = np.ascontiguousarray(np.repeat(np.asarray(inp["L0_ffn_qweight"]), 8, axis=0))
    scz = np.ascontiguousarray(np.concatenate(
        [_cols(inp["L0_ffn_scales"]), _cols(inp["L0_ffn_zeros"]),
         _cols(inp["L0_ffn_bias"]), _cols(inp["L0_attn_Wo_b"])], axis=1))
    shifts = ((np.arange(128, dtype=np.int32) % 8) * 4).reshape(128, 1)

    wqT1 = np.ascontiguousarray(np.asarray(inp["L1_attn_Wq"], dtype=np.float32).T)
    wkvT1 = np.ascontiguousarray(np.concatenate(
        [np.asarray(inp["L1_attn_Wk"], dtype=np.float32), np.asarray(inp["L1_attn_Wv"], dtype=np.float32)], axis=0).T)
    woT1 = np.ascontiguousarray(np.asarray(inp["L1_attn_Wo"], dtype=np.float32).T)
    wbT1 = np.ascontiguousarray(np.asarray(inp["L1_ffn_W_base"], dtype=np.float32).T)
    bco = _cols(inp["L1_ffn_b_base"])
    wgT1 = np.ascontiguousarray(np.asarray(inp["L1_ffn_W_gate"], dtype=np.float32).T)
    wtr1 = _cols(inp["L1_ffn_W_thr"])
    btr1 = np.ascontiguousarray(np.asarray(inp["L1_ffn_b_thr"], dtype=np.float32).reshape(1, 1))
    aaT = np.ascontiguousarray(np.asarray(inp["L1_ffn_A_w"], dtype=np.float32).transpose(2, 0, 1).reshape(D, 128))
    baT = np.ascontiguousarray(np.asarray(inp["L1_ffn_B_w"], dtype=np.float32).transpose(0, 2, 1).reshape(128, D))
    eex = np.zeros((8, 128), dtype=ml_dtypes.bfloat16)
    for ee in range(8):
        eex[ee, ee * 16:(ee + 1) * 16] = 1.0

    DH = 128
    inv_freq = 1.0 / (10000.0 ** (np.arange(0, DH, 2, dtype=np.float32) / DH))
    cosKs, sinKss, alphas, sigbs, hmasks = [], [], [], [], []
    base = np.linspace(0.0, 1.0, 256, dtype=np.float32)
    for c in range(NC):
        s0 = c * T
        pos = np.arange(s0, s0 + T, dtype=np.float32)
        f = np.outer(pos, inv_freq)
        cos = np.zeros((T, DH), dtype=np.float32)
        sin = np.zeros((T, DH), dtype=np.float32)
        cos[:, 0::2] = np.cos(f); cos[:, 1::2] = np.cos(f)
        sin[:, 0::2] = np.sin(f); sin[:, 1::2] = np.sin(f)
        sgn = np.ones((DH, 1), dtype=np.float32); sgn[:64] = -1.0
        cosKs.append(np.ascontiguousarray(cos.T).astype(ml_dtypes.bfloat16))
        sinKss.append(np.ascontiguousarray(sin.T * sgn).astype(ml_dtypes.bfloat16))
        al = np.zeros((128, T), dtype=np.float32)
        for b in range(8):
            al[:, b * 256:(b + 1) * 256] = base[None, :]
        if c == 0:
            al[:, 0:256] = 1.0
        alphas.append(np.ascontiguousarray(al).astype(ml_dtypes.bfloat16))
        sg = np.zeros((128, 64), dtype=np.float32)
        if c == NC - 1:
            for kc in (2, 3):
                sg[:, 7 * 8 + 1 * 4 + kc] = -1e9
        sigbs.append(sg)
        hm = np.zeros((128, 16), dtype=np.float32)
        if c > 0:
            hm[:, c - 1] = 1.0
        if c < NC - 1:
            hm[:, 8 + c + 1] = 1.0
        hmasks.append(hm)

    nc, stk = build_fused()
    in_maps = [dict(
        x=np.ascontiguousarray(x[c * T:(c + 1) * T]), wqkvT=wqkvT, bias12=bias12,
        qwr=qwr, scz=scz, woT0=woT0, shifts=shifts,
        wqT1=wqT1, wkvT1=wkvT1, cosK=cosKs[c], sinKs=sinKss[c],
        sigb=sigbs[c], alph=alphas[c], hmask=hmasks[c],
        woT1=woT1, wbT1=wbT1, bco=bco, wgT1=wgT1, wtr=wtr1, btr=btr1,
        aaT=aaT, baT=baT, eex=eex) for c in range(NC)]
    rr = run_bass_kernel_spmd(nc, in_maps, core_ids=list(range(NC))).results
    stk.close()
    out = np.concatenate([np.asarray(r["out_o"], dtype=np.float32) for r in rr], axis=0)
    return out.reshape(1, S, D)
